# version 02 (final)
# baseline (speedup 1.0000x reference)
import contextlib
import numpy as np
import concourse.bass as bass
import concourse.mybir as mybir
from concourse.bass_utils import run_bass_kernel_spmd

F32 = mybir.dt.float32
BF16 = mybir.dt.bfloat16
AF = mybir.ActivationFunctionType
ALU = mybir.AluOpType
AX = mybir.AxisListType

SEQ = 2048
DM = 1024
NH = 16
DK = 64
NCHUNK = 8
NTT = 4
NTQ = 16
RW = 4104
MW = 3968
NMC = 40
NUNIT = 20

BUILD_UPTO = "full"


class _R:
    def __init__(self, eng):
        object.__setattr__(self, "_e", eng)
        object.__setattr__(self, "last", None)

    def __getattr__(self, name):
        a = getattr(self._e, name)
        if callable(a):
            def call(*args, **kw):
                r = a(*args, **kw)
                if hasattr(r, "then_inc"):
                    try:
                        seq_only = r.ins.is_sequencer_only()
                    except Exception:
                        seq_only = False
                    if not seq_only:
                        object.__setattr__(self, "last", r)
                return r
            return call
        return a


class Sem:
    def __init__(self, nc, ctx, name):
        self.h = ctx.enter_context(nc.semaphore(name))
        self.n = 0

    def inc(self, obj, k=1):
        instr = obj.last if isinstance(obj, _R) else obj
        instr.then_inc(self.h, k)
        self.n += k
        return self.n


def build_nc():
    nc = bass.Bass("TRN2", target_bir_lowering=False, detect_race_conditions=False)

    x_d = nc.dram_tensor("x", [SEQ, DM], F32, kind="ExternalInput")
    wq_d = nc.dram_tensor("wq", [DM, DM], F32, kind="ExternalInput")
    wk_d = nc.dram_tensor("wk", [DM, DM], F32, kind="ExternalInput")
    wv_d = nc.dram_tensor("wv", [DM, DM], F32, kind="ExternalInput")
    wo_d = nc.dram_tensor("wo", [DM, DM], F32, kind="ExternalInput")
    bq_d = nc.dram_tensor("bq", [DM], F32, kind="ExternalInput")
    bk_d = nc.dram_tensor("bk", [DM], F32, kind="ExternalInput")
    bv_d = nc.dram_tensor("bv", [DM], F32, kind="ExternalInput")
    bo_d = nc.dram_tensor("bo", [DM], F32, kind="ExternalInput")
    relb_d = nc.dram_tensor("relb", [NH, 257], F32, kind="ExternalInput")
    rw_d = nc.dram_tensor("rw", [DM, NH], F32, kind="ExternalInput")
    wgr_d = nc.dram_tensor("wgr", [16, DM, 256], F32, kind="ExternalInput")
    wur_d = nc.dram_tensor("wur", [16, DM, 256], F32, kind="ExternalInput")
    wdr_d = nc.dram_tensor("wdr", [16, 256, DM], F32, kind="ExternalInput")
    wgs_d = nc.dram_tensor("wgs", [2, DM, 512], F32, kind="ExternalInput")
    wus_d = nc.dram_tensor("wus", [2, DM, 512], F32, kind="ExternalInput")
    wds_d = nc.dram_tensor("wds", [2, 512, DM], F32, kind="ExternalInput")
    ident_d = nc.dram_tensor("ident", [128, 128], F32, kind="ExternalInput")
    rev_d = nc.dram_tensor("rev", [128, 128], F32, kind="ExternalInput")
    identb_d = nc.dram_tensor("identb", [128, 128], F32, kind="ExternalInput")
    y_d = nc.dram_tensor("y", [SEQ, DM], F32, kind="ExternalOutput")

    bm_d = nc.dram_tensor("bm", [NH, 128, MW], F32, kind="ExternalInput")
    oh_d = nc.dram_tensor("oh", [16, 2048], F32, kind="ExternalInput")

    ctx = contextlib.ExitStack()
    with ctx:
        E = ctx.enter_context
        s_sb = E(nc.sbuf_tensor("s_sb", [128, NCHUNK, SEQ], F32))
        xb_sb = E(nc.sbuf_tensor("xb_sb", [128, NCHUNK, SEQ], BF16))
        ident_sb = E(nc.sbuf_tensor("ident_sb", [128, 128], F32))
        rev_sb = E(nc.sbuf_tensor("rev_sb", [128, 128], BF16))
        identb_sb = E(nc.sbuf_tensor("identb_sb", [128, 128], BF16))
        maxbT_sb = E(nc.sbuf_tensor("maxbT_sb", [1, 16], F32))
        oh_sb = E(nc.sbuf_tensor("oh_sb", [16, 16, 128], BF16))
        ones_sb = E(nc.sbuf_tensor("ones_sb", [128, 128], BF16))
        onesf_sb = E(nc.sbuf_tensor("onesf_sb", [128, 128], F32))
        ind_sb = E(nc.sbuf_tensor("ind_sb", [128, 2], BF16))
        bst_sb = E(nc.sbuf_tensor("bst_sb", [128, 5, 8], F32))
        rw_sb = E(nc.sbuf_tensor("rw_sb", [128, NCHUNK, NH], F32))
        combT_sb = E(nc.sbuf_tensor("combT_sb", [16, SEQ], BF16))
        fsc_sb = E(nc.sbuf_tensor("fsc_sb", [1, 16], F32))
        aFs = Sem(nc, ctx, "aFs")

        gM = Sem(nc, ctx, "gM")
        gFAp = [Sem(nc, ctx, "gFA0"), Sem(nc, ctx, "gFA1")]
        gFBp = [Sem(nc, ctx, "gFB0"), Sem(nc, ctx, "gFB1")]
        sXp = [Sem(nc, ctx, "sX0"), Sem(nc, ctx, "sX1")]
        sW = Sem(nc, ctx, "sW")
        sWp = [Sem(nc, ctx, "sWo0"), Sem(nc, ctx, "sWo1")]
        sOe = Sem(nc, ctx, "sOe")
        sOutp = [Sem(nc, ctx, "sOut0"), Sem(nc, ctx, "sOut1")]
        tTp = Sem(nc, ctx, "tTp")
        tPj = Sem(nc, ctx, "tPj")
        tSt = Sem(nc, ctx, "tSt")
        tBc = Sem(nc, ctx, "tBc")
        tQK = Sem(nc, ctx, "tQK")
        tAV = Sem(nc, ctx, "tAV")
        tRt = Sem(nc, ctx, "tRt")
        tTr = Sem(nc, ctx, "tTr")
        tFg = Sem(nc, ctx, "tFg")
        tCb = Sem(nc, ctx, "tCb")
        tFd = Sem(nc, ctx, "tFd")
        tTo = Sem(nc, ctx, "tTo")
        aEv = Sem(nc, ctx, "aEv")
        aSq = Sem(nc, ctx, "aSq")
        aEx = Sem(nc, ctx, "aEx")
        aR2 = Sem(nc, ctx, "aR2")
        aCp = Sem(nc, ctx, "aCp")
        aEx2 = Sem(nc, ctx, "aEx2")
        aSl = Sem(nc, ctx, "aSl")
        aR = Sem(nc, ctx, "aR")
        vS = Sem(nc, ctx, "vS")
        vWt = Sem(nc, ctx, "vWt")
        vVt = Sem(nc, ctx, "vVt")
        vSt = Sem(nc, ctx, "vSt")
        vR1 = Sem(nc, ctx, "vR1")
        vRo = Sem(nc, ctx, "vRo")
        vCb = Sem(nc, ctx, "vCb")
        vPm = Sem(nc, ctx, "vPm")
        vRl = Sem(nc, ctx, "vRl")
        vOe = Sem(nc, ctx, "vOe")
        vRes = Sem(nc, ctx, "vRes")
        vMis = Sem(nc, ctx, "vMis")
        vSt2 = Sem(nc, ctx, "vSt2")
        vRow = Sem(nc, ctx, "vRow")
        vAp = Sem(nc, ctx, "vAp")
        vTop1 = Sem(nc, ctx, "vTop1")
        vTop2 = Sem(nc, ctx, "vTop2")
        vTop3 = Sem(nc, ctx, "vTop3")
        vCe = Sem(nc, ctx, "vCe")
        vH = Sem(nc, ctx, "vH")
        vRes2 = Sem(nc, ctx, "vRes2")
        vOt = Sem(nc, ctx, "vOt")
        vF = Sem(nc, ctx, "vF")
        gMap = [Sem(nc, ctx, "gMa0"), Sem(nc, ctx, "gMa1")]
        sDbg = Sem(nc, ctx, "sDbg")
        tRv = Sem(nc, ctx, "tRv")
        vRv = Sem(nc, ctx, "vRv")

        L_rb16_tok = [0]

        def vfence(v):
            vF.inc(v.memset(fsc_sb[0:1, 0:8], 0.0))
            v.wait_ge(vF.h, vF.n)

        def vfence_carry(v, *carry):
            vF.inc(v.memset(fsc_sb[0:1, 0:8], 0.0))
            v.wait_ge(vF.h, vF.n)
            for s_ in carry:
                s_.inc(v.memset(fsc_sb[0:1, 0:8], 0.0))

        def afence_carry(s, *carry):
            aFs.inc(s.copy(fsc_sb[0:1, 8:16], fsc_sb[0:1, 8:16]))
            s.wait_ge(aFs.h, aFs.n)
            for s_ in carry:
                s_.inc(s.copy(fsc_sb[0:1, 8:16], fsc_sb[0:1, 8:16]))

        def afence(s):
            aFs.inc(s)
            s.wait_ge(aFs.h, aFs.n)
            s.copy(fsc_sb[0:1, 8:16], fsc_sb[0:1, 8:16])

        with contextlib.ExitStack() as c0:
            E0 = c0.enter_context
            xtm_sb = E0(nc.sbuf_tensor("xtm_sb", [128, 2, 1024], F32))
            brow_sb = E0(nc.sbuf_tensor("brow_sb", [4, 1024], F32))
            relb_sb = E0(nc.sbuf_tensor("relb_sb", [16, 257], F32))
            exprelb_sb = E0(nc.sbuf_tensor("exprelb_sb", [16, 257], F32))
            tp_ps = E0(nc.psum_tensor("tp_ps", [128, 2, 512], F32))

            with nc.Block("b0") as blk:
                def _g(g):
                    g = _R(g)
                    g.dma_start(out=ident_sb[:], in_=ident_d[:, :]); gM.inc(g, 16)
                    g.dma_start(out=rev_sb[:], in_=rev_d[:, :]); gM.inc(g, 16)
                    g.dma_start(out=identb_sb[:], in_=identb_d[:, :]); gM.inc(g, 16)
                    g.dma_start(out=oh_sb[:], in_=oh_d[:, :]); gM.inc(g, 16)
                    g.dma_start(out=relb_sb[:], in_=relb_d[:, :]); gM.inc(g, 16)
                    for bi, bd in enumerate([bq_d, bk_d, bv_d, bo_d]):
                        g.dma_start(out=brow_sb[bi:bi + 1, :], in_=bd[:]); gM.inc(g, 16)
                    src = bass.AP(rw_d, 0, [[NH, 128], [128 * NH, 8], [1, NH]])
                    g.dma_start(out=rw_sb[:], in_=src, allow_slow_non_contiguous=True); gM.inc(g, 16)
                    g.memset(ones_sb[:], 1.0)
                    g.memset(fsc_sb[:], 0.0)
                    g.memset(onesf_sb[:], 1.0)
                    g.wait_ge(gM.h, gM.n)
                blk.gpsimd(_g)
                tok_init = 160



                def _v(v):
                    v = _R(v)
                    v.memset(ind_sb[:], 0.0)
                    vfence(v)
                    v.memset(ind_sb[0:64, 0:1], 1.0)
                    v.memset(ind_sb[64:128, 1:2], 1.0)
                    v.wait_ge(gM.h, tok_init)
                    v.reduce_max(exprelb_sb[:, 0:1], relb_sb[:], axis=AX.X,
                                 apply_absolute_value=True)
                    vfence_carry(v, vMis)
                blk.vector(_v)

                tTp0 = tTp.n
                vS0, aCp0 = vS.n, aCp.n

                def _sx(sy):
                    sy = _R(sy)
                    for tq in range(NTQ):
                        if tq >= 2:
                            sy.wait_ge(tTp.h, tTp0 + 8 * (tq - 1))
                        sy.dma_start(out=xtm_sb[:, tq % 2, :], in_=x_d[tq * 128:(tq + 1) * 128, :])
                        sXp[tq % 2].inc(sy, 16)
                blk.sync(_sx)

                def _t(t):
                    t = _R(t)
                    t.wait_ge(gM.h, tok_init)
                    for tq in range(NTQ):
                        t.wait_ge(sXp[tq % 2].h, 16 * (tq // 2 + 1))
                        for c in range(NCHUNK):
                            i = tq * 8 + c
                            if i >= 2:
                                t.wait_ge(vS.h, vS0 + i - 1)
                                t.wait_ge(aCp.h, aCp0 + i - 1)
                            t.transpose(tp_ps[:, i % 2, 0:128], xtm_sb[:, tq % 2, c * 128:(c + 1) * 128],
                                        ident_sb[:])
                            tTp.inc(t)
                    tTp.inc(t.nop())
                    t.wait_ge(vS.h, vS0 + 128)
                    t.wait_ge(aCp.h, aCp0 + 128)
                    t.wait_ge(vMis.h, 1)
                    t.transpose(tp_ps[0:1, 0, 4:20], exprelb_sb[0:16, 0:1],
                                ident_sb[0:16, 0:16])
                    tTp.inc(t)
                    for jb in range(8):
                        if jb >= 2:
                            t.wait_ge(vMis.h, jb)
                        t.transpose(tp_ps[:, jb % 2, 0:4], brow_sb[0:4, jb * 128:(jb + 1) * 128],
                                    ident_sb[0:4, 0:4])
                        tTp.inc(t)
                    tTp.inc(t.nop())
                blk.tensor(_t)

                def _v2(v):
                    v = _R(v)
                    for tq in range(NTQ):
                        for c in range(NCHUNK):
                            i = tq * 8 + c
                            v.wait_ge(tTp.h, tTp0 + i + 2)
                            v.tensor_copy(s_sb[:, c, tq * 128:(tq + 1) * 128], tp_ps[:, i % 2, 0:128])
                            vS.inc(v)
                    v.wait_ge(tTp.h, tTp0 + 131)
                    v.tensor_copy(maxbT_sb[:], tp_ps[0:1, 0, 4:20])
                    for jb in range(8):
                        v.wait_ge(tTp.h, tTp0 + min(132 + jb, 139))
                        v.tensor_copy(bst_sb[:, 0:4, jb], tp_ps[:, jb % 2, 0:4])
                        vMis.inc(v)
                    vfence(v)
                    v.tensor_scalar_mul(bst_sb[:, 4, :], bst_sb[:, 0, :], 0.125)
                    vMis.inc(v)
                blk.vector(_v2)

                def _s2(s):
                    s = _R(s)
                    for tq in range(NTQ):
                        for c in range(NCHUNK):
                            i = tq * 8 + c
                            s.wait_ge(vS.h, vS0 + i + 1)
                            s.activation(xb_sb[:, c, tq * 128:(tq + 1) * 128],
                                         s_sb[:, c, tq * 128:(tq + 1) * 128], AF.Identity)
                            aCp.inc(s)
                blk.scalar(_s2)

        if BUILD_UPTO != "x":
            with contextlib.ExitStack() as ca:
                EA = ca.enter_context
                obuf_sb = EA(nc.sbuf_tensor("obuf_sb", [128, NCHUNK, SEQ], BF16))
                wst_sb = EA(nc.sbuf_tensor("wst_sb", [128, 2, 1024], F32))
                WT_sb = EA(nc.sbuf_tensor("WT_sb", [128, 2, 8, 128], BF16))
                Qc_sb = EA(nc.sbuf_tensor("Qc_sb", [128, SEQ], BF16))
                Kc_sb = EA(nc.sbuf_tensor("Kc_sb", [128, SEQ], BF16))
                Vf_sb = EA(nc.sbuf_tensor("Vf_sb", [128, SEQ], F32))
                VT_sb = EA(nc.sbuf_tensor("VT_sb", [128, 16, 130], BF16))
                sq_sb = EA(nc.sbuf_tensor("sq_sb", [128, SEQ], BF16))
                E_sb = wst_sb.bitcast(BF16).reshape([128, 4, 1024])
                mst_sb = EA(nc.sbuf_tensor("mst_sb", [128, 2, MW], BF16))
                otmp_sb = EA(nc.sbuf_tensor("otmp_sb", [64, 2, 1024], BF16))
                rlsb_sb = EA(nc.sbuf_tensor("rlsb_sb", [128, 1024], F32))
                rlf_sb = EA(nc.sbuf_tensor("rlf_sb", [65, 1024], F32))
                cb_sb = EA(nc.sbuf_tensor("cb_sb", [128, 4], F32))
                cwork_sb = EA(nc.sbuf_tensor("cwork_sb", [1, 32], F32))
                ncb_sb = EA(nc.sbuf_tensor("ncb_sb", [1, 4], BF16))

                with nc.Block("bm0") as blk:
                    _emit_master_load(nc, {"bm_d": bm_d, "mst_sb": mst_sb, "gMap": gMap}, blk, 0)

                for c in range(NCHUNK):
                    _emit_proj_chunk(nc, c, locals())
                    _emit_attn_head(nc, 2 * c, locals())
                    _emit_attn_head(nc, 2 * c + 1, locals())

                _emit_wo(nc, locals())

            with contextlib.ExitStack() as cl:
                _emit_ln(nc, locals(), cl, final=(BUILD_UPTO == "attn"))

            if BUILD_UPTO not in ("attn", "ln1"):
                with contextlib.ExitStack() as cr:
                    _emit_router(nc, locals(), cr)
                with contextlib.ExitStack() as cf:
                    _emit_ffn(nc, locals(), cf)
                with contextlib.ExitStack() as cl2:
                    _emit_ln(nc, locals(), cl2, final=True)
            elif BUILD_UPTO == "ln1":
                pass

        with contextlib.ExitStack() as co:
            EO = co.enter_context
            otm_sb = EO(nc.sbuf_tensor("otm_sb", [128, 2, 1024], F32))
            to_ps = EO(nc.psum_tensor("to_ps", [128, 2, 512], F32))
            with nc.Block("bout") as blk:
                tTo0, vOt0 = tTo.n, vOt.n
                sOut0p = [sOutp[0].n, sOutp[1].n]

                def _t(t):
                    t = _R(t)
                    for tq in range(NTQ):
                        for c in range(NCHUNK):
                            i = tq * 8 + c
                            if i >= 2:
                                t.wait_ge(vOt.h, vOt0 + i - 1)
                            t.transpose(to_ps[:, i % 2, 0:128], s_sb[:, c, tq * 128:(tq + 1) * 128],
                                        ident_sb[:])
                            tTo.inc(t)
                blk.tensor(_t)

                def _v(v):
                    v = _R(v)
                    for tq in range(NTQ):
                        if tq >= 2:
                            v.wait_ge(sOutp[tq % 2].h, sOut0p[tq % 2] + 16 * (tq // 2))
                        for c in range(NCHUNK):
                            i = tq * 8 + c
                            v.wait_ge(tTo.h, tTo0 + i + 1)
                            v.tensor_copy(otm_sb[:, tq % 2, c * 128:(c + 1) * 128], to_ps[:, i % 2, 0:128])
                            vOt.inc(v)
                blk.vector(_v)

                def _sy(sy):
                    sy = _R(sy)
                    for tq in range(NTQ):
                        sy.wait_ge(vOt.h, vOt0 + 8 * (tq + 1))
                        sy.dma_start(out=y_d[tq * 128:(tq + 1) * 128, :], in_=otm_sb[:, tq % 2, :])
                        sOutp[tq % 2].inc(sy, 16)
                    sy.wait_ge(sOutp[0].h, sOutp[0].n)
                    sy.wait_ge(sOutp[1].h, sOutp[1].n)
                blk.sync(_sy)

    return nc





def _emit_master_load(nc, L, blk, head):
    bm_d, mst_sb = L["bm_d"], L["mst_sb"]
    gMap = L["gMap"]
    par = head % 2

    def _g(g):
        g = _R(g)
        g.dma_start(out=mst_sb[:, par, :], in_=bm_d[head, :, :])
        gMap[par].inc(g, 16)
    blk.gpsimd(_g)


def _emit_proj_chunk(nc, c, L):
    g = lambda n: L[n]
    (s_sb, xb_sb, ident_sb, ones_sb, ind_sb, bst_sb) = (L["s_sb"], L["xb_sb"], L["ident_sb"],
                                                        L["ones_sb"], L["ind_sb"], L["bst_sb"])
    (wst_sb, WT_sb, Qc_sb, Kc_sb, Vf_sb, VT_sb, sq_sb, mst_sb,
     cb_sb, cwork_sb, ncb_sb) = (L["wst_sb"], L["WT_sb"], L["Qc_sb"], L["Kc_sb"],
                                 L["Vf_sb"], L["VT_sb"], L["sq_sb"], L["mst_sb"],
                                 L["cb_sb"], L["cwork_sb"], L["ncb_sb"])
    maxbT_sb = L["maxbT_sb"]
    afence = L["afence"]
    (wq_d, wk_d, wv_d) = (L["wq_d"], L["wk_d"], L["wv_d"])
    sW, tTp, tPj, tSt, tBc = L["sW"], L["tTp"], L["tPj"], L["tSt"], L["tBc"]
    aEv, aSq, aR2 = L["aEv"], L["aSq"], L["aR2"]
    vWt, vVt, vSt, vR1, vRo, vCb = L["vWt"], L["vVt"], L["vSt"], L["vR1"], L["vRo"], L["vCb"]
    vfence = L["vfence"]
    vfence_carry = L["vfence_carry"]

    with contextlib.ExitStack() as cp:
        EP = cp.enter_context
        tp_ps = EP(nc.psum_tensor("tp_ps_%d" % c, [128, 2, 512], F32))
        pj_ps = EP(nc.psum_tensor("pj_ps_%d" % c, [128, 2, 512], F32))
        st_ps = EP(nc.psum_tensor("st_ps_%d" % c, [1, 2, 512], F32))
        cb_ps = EP(nc.psum_tensor("cb_ps_%d" % c, [128, 512], F32))

        sW0, tTp0, tPj0, tSt0, tBc0 = sW.n, tTp.n, tPj.n, tSt.n, tBc.n
        aEv0, aSq0, aR20 = aEv.n, aSq.n, aR2.n
        vWt0, vVt0, vSt0, vR10, vRo0, vCb0 = vWt.n, vVt.n, vSt.n, vR1.n, vRo.n, vCb.n

        with nc.Block("proj%d" % c) as blk:
            def _sy(sy):
                sy = _R(sy)
                for wi, wd in enumerate([wq_d, wk_d, wv_d]):
                    if wi == 2:
                        sy.wait_ge(tTp.h, tTp0 + 8)
                    sy.dma_start(out=wst_sb[:, wi % 2, :], in_=wd[c * 128:(c + 1) * 128, :])
                    sW.inc(sy, 16)
            blk.sync(_sy)


            def _t(t):
                t = _R(t)
                if c == 0:
                    pass
                for wi in range(3):
                    t.wait_ge(sW.h, sW0 + 16 * (wi + 1) if wi == 2 else sW0 + 32)
                    for jb in range(8):
                        sl = wi * 8 + jb
                        if sl >= 2:
                            t.wait_ge(vWt.h, vWt0 + sl - 1)
                        t.transpose(tp_ps[:, sl % 2, 0:128], wst_sb[:, wi % 2, jb * 128:(jb + 1) * 128],
                                    ident_sb[:])
                        tTp.inc(t)
                    t.wait_ge(vWt.h, vWt0 + 8 * (wi + 1))
                    for tt in range(NTT):
                        s2 = wi * 4 + tt
                        if s2 >= 2:
                            t.wait_ge(aEv.h, aEv0 + s2 - 1)
                        for jc in range(8):
                            t.matmul(pj_ps[:, tt % 2, :], WT_sb[:, wi % 2, jc, :],
                                     xb_sb[:, jc, tt * 512:(tt + 1) * 512],
                                     start=(jc == 0), stop=(jc == 7))
                        tPj.inc(t)
                t.wait_ge(aEv.h, aEv0 + 12)
                t.wait_ge(vWt.h, vWt0 + 24)
                for kt in range(16):
                    if kt >= 2:
                        t.wait_ge(vVt.h, vVt0 + kt - 1)
                    t.transpose(tp_ps[:, kt % 2, 0:128], Vf_sb[:, kt * 128:(kt + 1) * 128], ident_sb[:])
                    tTp.inc(t)
                for qi in range(2):
                    t.wait_ge(aSq.h, aSq0 + qi + 1)
                    for half in range(2):
                        for tt in range(NTT):
                            si = qi * 8 + half * 4 + tt
                            if si >= 2:
                                t.wait_ge(vSt.h, vSt0 + si - 1)
                            t.matmul(st_ps[:, si % 2, :], ind_sb[:, half:half + 1],
                                     sq_sb[:, tt * 512:(tt + 1) * 512], start=True, stop=True)
                            tSt.inc(t)
                t.wait_ge(vRo.h, vRo0 + 1)
                t.matmul(cb_ps[:, 0:4], ones_sb[0:1, :], ncb_sb[0:1, 0:4], start=True, stop=True)
                tBc.inc(t)
            blk.tensor(_t)

            def _s(s):
                s = _R(s)
                for wi in range(3):
                    for tt in range(NTT):
                        s2 = wi * 4 + tt
                        s.wait_ge(tPj.h, tPj0 + s2 + 1)
                        win = slice(tt * 512, (tt + 1) * 512)
                        if wi == 0:
                            s.activation(Qc_sb[:, win], pj_ps[:, tt % 2, :], AF.Identity,
                                         bias=bst_sb[:, 4, c:c + 1], scale=0.125)
                        elif wi == 1:
                            s.activation(Kc_sb[:, win], pj_ps[:, tt % 2, :], AF.Identity,
                                         bias=bst_sb[:, 1, c:c + 1])
                        else:
                            s.activation(Vf_sb[:, win], pj_ps[:, tt % 2, :], AF.Identity,
                                         bias=bst_sb[:, 2, c:c + 1])
                        aEv.inc(s)
                s.activation(sq_sb[:], Qc_sb[:], AF.Square)
                afence(s); aSq.inc(s)
                s.wait_ge(tSt.h, tSt0 + 8)
                s.activation(sq_sb[:], Kc_sb[:], AF.Square)
                afence(s); aSq.inc(s)
                s.wait_ge(vR1.h, vR10 + 1)
                s.activation(cwork_sb[0:1, 24:28], cwork_sb[0:1, 0:4], AF.Sqrt); aR2.inc(s)
            blk.scalar(_s)

            def _v(v):
                v = _R(v)
                for wi in range(3):
                    if wi >= 1:
                        v.wait_ge(tPj.h, tPj0 + 4 * wi)
                    for jb in range(8):
                        sl = wi * 8 + jb
                        v.wait_ge(tTp.h, tTp0 + sl + 1)
                        v.tensor_copy(WT_sb[:, wi % 2, jb, :], tp_ps[:, sl % 2, 0:128])
                        vWt.inc(v)
                if c == 0:
                    v.memset(VT_sb[:, :, 64:65], 1.0)
                    v.memset(VT_sb[:, :, 129:130], 1.0)
                for kt in range(16):
                    v.wait_ge(tTp.h, tTp0 + 24 + kt + 1)
                    v.tensor_copy(VT_sb[:, kt, 0:64], tp_ps[:, kt % 2, 0:64])
                    v.tensor_copy(VT_sb[:, kt, 65:129], tp_ps[:, kt % 2, 64:128])
                    vVt.inc(v)
                for si in range(16):
                    v.wait_ge(tSt.h, tSt0 + si + 1)
                    v.reduce_max(cwork_sb[0:1, 8 + si:9 + si], st_ps[0:1, si % 2, :], axis=AX.X)
                    vSt.inc(v)
                vfence(v)
                for half in range(2):
                    for w in range(2):
                        v.tensor_max(cwork_sb[0:1, half * 2 + w:half * 2 + w + 1],
                                     cwork_sb[0:1, 8 + half * 4 + 2 * w:9 + half * 4 + 2 * w],
                                     cwork_sb[0:1, 9 + half * 4 + 2 * w:10 + half * 4 + 2 * w])
                    v.tensor_max(cwork_sb[0:1, 4 + half:5 + half],
                                 cwork_sb[0:1, 16 + half * 4:17 + half * 4],
                                 cwork_sb[0:1, 17 + half * 4:18 + half * 4])
                    v.tensor_max(cwork_sb[0:1, 6 + half:7 + half],
                                 cwork_sb[0:1, 18 + half * 4:19 + half * 4],
                                 cwork_sb[0:1, 19 + half * 4:20 + half * 4])
                    vfence(v)
                    v.tensor_max(cwork_sb[0:1, 4 + half:5 + half],
                                 cwork_sb[0:1, 4 + half:5 + half],
                                 cwork_sb[0:1, 6 + half:7 + half])
                vfence(v)
                for half in range(2):
                    for w in range(2):
                        j = half * 2 + w
                        v.tensor_mul(cwork_sb[0:1, j:j + 1], cwork_sb[0:1, j:j + 1],
                                     cwork_sb[0:1, 4 + half:5 + half])
                vfence_carry(v, vR1)
                v.wait_ge(aR2.h, aR20 + 1)
                v.tensor_single_scalar(cwork_sb[0:1, 24:26], cwork_sb[0:1, 24:26],
                                       maxbT_sb[0:1, 2 * c:2 * c + 1], op=ALU.add)
                v.tensor_single_scalar(cwork_sb[0:1, 26:28], cwork_sb[0:1, 26:28],
                                       maxbT_sb[0:1, 2 * c + 1:2 * c + 2], op=ALU.add)
                vfence(v)
                v.tensor_scalar_mul(cwork_sb[0:1, 28:32], cwork_sb[0:1, 24:28], -1.0)
                vfence(v)
                v.tensor_copy(ncb_sb[:], cwork_sb[0:1, 28:32])
                vfence_carry(v, vRo)
                v.wait_ge(tBc.h, tBc0 + 1)
                v.tensor_copy(cb_sb[:], cb_ps[:, 0:4])
                vCb.inc(v)
            blk.vector(_v)


def _emit_attn_head(nc, h, L):
    c, half = h // 2, h % 2
    (obuf_sb, Qc_sb, Kc_sb, VT_sb, E_sb, mst_sb, otmp_sb, rlsb_sb, rlf_sb,
     cb_sb, ones_sb, onesf_sb) = (L["obuf_sb"], L["Qc_sb"], L["Kc_sb"], L["VT_sb"], L["E_sb"],
                                L["mst_sb"], L["otmp_sb"], L["rlsb_sb"], L["rlf_sb"],
                                L["cb_sb"], L["ones_sb"], L["onesf_sb"])
    identb_sb = L["identb_sb"]
    tQK, tAV, tBc, aEx = L["tQK"], L["tAV"], L["tBc"], L["aEx"]
    vPm, vRl, vOe, sOe = L["vPm"], L["vRl"], L["vOe"], L["sOe"]
    rows = slice(64 * half, 64 * half + 64)
    vtc = slice(65 * half, 65 * half + 65)
    mpar = h % 2

    with nc.Block("attn%d" % h) as blk:
      with contextlib.ExitStack() as chp:
        aux_ps = chp.enter_context(nc.psum_tensor("aux_ps_%d" % h, [128, 2, 512], F32))
        for qt2 in range(2):
            with contextlib.ExitStack() as cq:
                EQ = cq.enter_context
                S_ps = EQ(nc.psum_tensor("S_ps_%d_%d" % (h, qt2), [128, 2, 1024], F32))
                av_ps = EQ(nc.psum_tensor("av_ps_%d_%d" % (h, qt2), [65, 1024], F32))
                rl_ps = aux_ps
                qwin = slice(qt2 * 1024, (qt2 + 1) * 1024)
                tQK0, tAV0, tBc0, aEx0 = tQK.n, tAV.n, tBc.n, aEx.n
                vPm0, vRl0, vOe0 = vPm.n, vRl.n, vOe.n

                def _t(t, h=h, qt2=qt2, tQK0=tQK0, tAV0=tAV0, aEx0=aEx0, vPm0=vPm0,
                       S_ps=S_ps, av_ps=av_ps, rl_ps=rl_ps, qwin=qwin):
                    t = _R(t)
                    if qt2 == 0:
                        t.wait_ge(L["gMap"][h % 2].h, 16 * (h // 2 + 1))
                    else:
                        t.wait_ge(vOe.h, vOe.n)
                    def av(kt):
                        t.wait_ge(aEx.h, aEx0 + kt + 2)
                        for sg in range(2):
                            t.matmul(av_ps[:, sg * 512:(sg + 1) * 512], VT_sb[:, kt, vtc],
                                     E_sb[:, kt % 2, sg * 512:(sg + 1) * 512],
                                     start=(kt == 0), stop=(kt == 15), skip_group_check=True)
                        tAV.inc(t)
                    for kt in range(16):
                        if kt >= 2:
                            t.wait_ge(aEx.h, aEx0 + kt - 1)
                        boff = mpar * MW + 1920 + 1024 * qt2 - 128 * kt
                        for sg in range(2):
                            t.matmul(S_ps[:, kt % 2, sg * 512:(sg + 1) * 512],
                                     Kc_sb[rows, kt * 128:(kt + 1) * 128],
                                     Qc_sb[rows, qt2 * 1024 + sg * 512: qt2 * 1024 + (sg + 1) * 512],
                                     start=True, stop=False, tile_position=(64 * half, 0),
                                     skip_group_check=True)
                            t.matmul(S_ps[:, kt % 2, sg * 512:(sg + 1) * 512],
                                     identb_sb[:],
                                     bass.AP(mst_sb, boff + sg * 512, [[2 * MW, 128], [1, 512]]),
                                     start=False, stop=True,
                                     skip_group_check=True)
                        tQK.inc(t)
                        if kt >= 1:
                            av(kt - 1)
                    av(15)
                    t.wait_ge(vRl.h, vRl0 + 1)
                    for sg in range(2):
                        t.matmul(rl_ps[:, sg, :], onesf_sb[64:65, :],
                                 rlf_sb[64:65, sg * 512:(sg + 1) * 512], start=True, stop=True,
                                 skip_group_check=True)
                    tBc.inc(t)
                blk.tensor(_t)

                def _s(s, h=h, qt2=qt2, tQK0=tQK0, tAV0=tAV0, aEx0=aEx0, S_ps=S_ps):
                    s = _R(s)
                    bcol = half * 2 + qt2
                    for kt in range(16):
                        s.wait_ge(tQK.h, tQK0 + kt + 1)
                        if kt >= 2:
                            s.wait_ge(tAV.h, tAV0 + kt - 1)
                        s.activation(E_sb[:, kt % 2, 0:512], S_ps[:, kt % 2, 0:512], AF.Exp,
                                     bias=cb_sb[:, bcol:bcol + 1])
                        s.activation(E_sb[:, kt % 2, 512:1024], S_ps[:, kt % 2, 512:1024], AF.Exp,
                                     bias=cb_sb[:, bcol:bcol + 1])
                        aEx.inc(s)
                    L["afence_carry"](s, aEx)
                blk.scalar(_s)

                def _v(v, h=h, qt2=qt2, aEx0=aEx0, tAV0=tAV0, tBc0=tBc0, vOe0=vOe0,
                       av_ps=av_ps, rl_ps=rl_ps, qwin=qwin):
                    v = _R(v)
                    v.wait_ge(tAV.h, tAV0 + 16)
                    v.reciprocal(rlf_sb[64:65, :], av_ps[64:65, :])
                    L["vfence_carry"](v, vRl)
                    v.wait_ge(tBc.h, tBc0 + 1)
                    v.tensor_copy(rlsb_sb[:, 0:512], rl_ps[:, 0, :])
                    v.tensor_copy(rlsb_sb[:, 512:1024], rl_ps[:, 1, :])
                    L["vfence"](v)
                    if half == 0:
                        v.tensor_mul(obuf_sb[0:64, c, qwin], av_ps[0:64, :], rlsb_sb[0:64, :])
                    else:
                        v.tensor_mul(otmp_sb[:, qt2, :], av_ps[0:64, :], rlsb_sb[0:64, :])
                    vOe.inc(v)
                blk.vector(_v)

                if half == 1:
                    def _sy(sy, qt2=qt2, vOe0=vOe0, qwin=qwin):
                        sy = _R(sy)
                        sy.wait_ge(vOe.h, vOe0 + 1)
                        sy.dma_start(out=obuf_sb[64:128, c, qwin], in_=otmp_sb[:, qt2, :])
                        sOe.inc(sy, 16)
                    blk.sync(_sy)
        if h < 15:
            _emit_master_load(nc, L, blk, h + 1)


def _emit_wo(nc, L):
    (s_sb, obuf_sb, wst_sb, WT_sb, ident_sb, bst_sb, wo_d) = (
        L["s_sb"], L["obuf_sb"], L["wst_sb"], L["WT_sb"], L["ident_sb"], L["bst_sb"], L["wo_d"])
    sWp, tTp, tPj = L["sWp"], L["tTp"], L["tPj"]
    vWt, vRes = L["vWt"], L["vRes"]

    with contextlib.ExitStack() as cw:
        EW = cw.enter_context
        tp_ps = EW(nc.psum_tensor("tp_ps_wo", [128, 2, 512], F32))
        wo_ps = EW(nc.psum_tensor("wo_ps", [128, 2, 512], F32))
        sWp0 = [sWp[0].n, sWp[1].n]
        tTp0, tPj0, vWt0, vRes0 = tTp.n, tPj.n, vWt.n, vRes.n

        with nc.Block("wo") as blk:
            def _sy(sy):
                sy = _R(sy)
                for i in range(8):
                    if i >= 2:
                        sy.wait_ge(tTp.h, tTp0 + 8 * (i - 1))
                    sy.dma_start(out=wst_sb[:, i % 2, :], in_=wo_d[i * 128:(i + 1) * 128, :])
                    sWp[i % 2].inc(sy, 16)
            blk.sync(_sy)

            def _t(t):
                t = _R(t)
                t.wait_ge(L["sOe"].h, L["sOe"].n)
                for i in range(8):
                    t.wait_ge(sWp[i % 2].h, sWp0[i % 2] + 16 * (i // 2 + 1))
                    for jb in range(8):
                        sl = i * 8 + jb
                        if sl >= 2:
                            t.wait_ge(vWt.h, vWt0 + sl - 1)
                        t.transpose(tp_ps[:, sl % 2, 0:128], wst_sb[:, i % 2, jb * 128:(jb + 1) * 128],
                                    ident_sb[:])
                        tTp.inc(t)
                    t.wait_ge(vWt.h, vWt0 + 8 * (i + 1))
                    for tt in range(NTT):
                        s2 = i * 4 + tt
                        if s2 >= 2:
                            t.wait_ge(vRes.h, vRes0 + s2 - 1)
                        for dc in range(8):
                            t.matmul(wo_ps[:, tt % 2, :], WT_sb[:, i % 2, dc, :],
                                     obuf_sb[:, dc, tt * 512:(tt + 1) * 512],
                                     start=(dc == 0), stop=(dc == 7))
                        tPj.inc(t)
            blk.tensor(_t)

            def _v(v):
                v = _R(v)
                for i in range(8):
                    if i >= 1:
                        v.wait_ge(tPj.h, tPj0 + 4 * i)
                    for jb in range(8):
                        sl = i * 8 + jb
                        v.wait_ge(tTp.h, tTp0 + sl + 1)
                        v.tensor_copy(WT_sb[:, i % 2, jb, :], tp_ps[:, sl % 2, 0:128])
                        vWt.inc(v)
                    for tt in range(NTT):
                        s2 = i * 4 + tt
                        v.wait_ge(tPj.h, tPj0 + s2 + 1)
                        win = slice(tt * 512, (tt + 1) * 512)
                        v.scalar_tensor_tensor(s_sb[:, i, win], wo_ps[:, tt % 2, :],
                                               bst_sb[:, 3, i:i + 1], s_sb[:, i, win],
                                               op0=ALU.add, op1=ALU.add)
                        vRes.inc(v)
            blk.vector(_v)


def _emit_ln(nc, L, cl, final):
    (s_sb, xb_sb, ones_sb, onesf_sb) = (L["s_sb"], L["xb_sb"], L["ones_sb"], L["onesf_sb"])
    vfence = L["vfence"]
    vfence_carry = L["vfence_carry"]
    tSt, tBc = L["tSt"], L["tBc"]
    aCp, aSq, aR2 = L["aCp"], L["aSq"], L["aR2"]
    vSt2, vRow, vAp = L["vSt2"], L["vRow"], L["vAp"]

    EL = cl.enter_context
    tag = "f" if final else "p"
    sq2_sb = EL(nc.sbuf_tensor("ln_sq_" + tag, [128, 4, 512], BF16))
    mrow_sb = EL(nc.sbuf_tensor("ln_m_" + tag, [1, SEQ], F32))
    qrow_sb = EL(nc.sbuf_tensor("ln_q_" + tag, [1, SEQ], F32))
    trow_sb = EL(nc.sbuf_tensor("ln_t_" + tag, [1, SEQ], F32))
    r0row_sb = EL(nc.sbuf_tensor("ln_r0_" + tag, [1, SEQ], F32))
    rrow_sb = EL(nc.sbuf_tensor("ln_r_" + tag, [1, SEQ], F32))
    tmpf_sb = EL(nc.sbuf_tensor("ln_tmp_" + tag, [128, 512], F32))
    sum_ps = EL(nc.psum_tensor("ln_sum_ps_" + tag, [1, 2, 512], F32))
    sqs_ps = EL(nc.psum_tensor("ln_sqs_ps_" + tag, [1, 2, 512], F32))
    m_ps = EL(nc.psum_tensor("ln_m_ps_" + tag, [128, 2, 512], F32))
    r_ps = EL(nc.psum_tensor("ln_r_ps_" + tag, [128, 2, 512], F32))

    tSt0, tBc0, aCp0, aSq0, aR20 = tSt.n, tBc.n, aCp.n, aSq.n, aR2.n
    vSt20, vRow0, vAp0 = vSt2.n, vRow.n, vAp.n

    with nc.Block("ln_" + tag) as blk:
        def _s(s):
            s = _R(s)
            for tt in range(NTT):
                for c in range(NCHUNK):
                    i = tt * 8 + c
                    win = slice(tt * 512, (tt + 1) * 512)
                    if i >= 4:
                        s.wait_ge(tSt.h, tSt0 + 2 * (i - 3))
                    s.activation(xb_sb[:, c, win], s_sb[:, c, win], AF.Identity)
                    aCp.inc(s)
                    s.activation(sq2_sb[:, i % 4, :], s_sb[:, c, win], AF.Square)
                    aSq.inc(s)
            L["afence_carry"](s, aCp, aSq)
            s.wait_ge(vRow.h, vRow0 + 1)
            s.activation(r0row_sb[:], trow_sb[:], AF.Sqrt)
            aR2.inc(s)
        blk.scalar(_s)

        def _t(t):
            t = _R(t)
            for tt in range(NTT):
                if tt >= 2:
                    t.wait_ge(vSt2.h, vSt20 + 2 * (tt - 1))
                for c in range(NCHUNK):
                    i = tt * 8 + c
                    t.wait_ge(aCp.h, aCp0 + min(i + 2, 33))
                    t.matmul(sum_ps[:, tt % 2, :], ones_sb[:, 0:1], xb_sb[:, c, tt * 512:(tt + 1) * 512],
                             start=(c == 0), stop=(c == 7), skip_group_check=True)
                    t.wait_ge(aSq.h, aSq0 + min(i + 2, 33))
                    t.matmul(sqs_ps[:, tt % 2, :], ones_sb[:, 0:1], sq2_sb[:, i % 4, :],
                             start=(c == 0), stop=(c == 7), skip_group_check=True)
                    tSt.inc(t, 2)
            for tt in range(NTT):
                t.wait_ge(vRow.h, vRow0 + 2)
                if tt >= 2:
                    t.wait_ge(vAp.h, vAp0 + 8 * (tt - 1))
                win = slice(tt * 512, (tt + 1) * 512)
                t.matmul(m_ps[:, tt % 2, :], onesf_sb[0:1, :], mrow_sb[0:1, win], start=True, stop=True,
                         skip_group_check=True)
                t.matmul(r_ps[:, tt % 2, :], onesf_sb[0:1, :], rrow_sb[0:1, win], start=True, stop=True,
                         skip_group_check=True)
                tBc.inc(t)
        blk.tensor(_t)

        def _v(v):
            v = _R(v)
            for tt in range(NTT):
                v.wait_ge(tSt.h, tSt0 + 16 * (tt + 1))
                win = slice(tt * 512, (tt + 1) * 512)
                v.tensor_copy(mrow_sb[0:1, win], sum_ps[0:1, tt % 2, :])
                v.tensor_copy(qrow_sb[0:1, win], sqs_ps[0:1, tt % 2, :])
                vSt2.inc(v, 2)
            v.tensor_scalar_mul(mrow_sb[:], mrow_sb[:], 1.0 / DM)
            v.tensor_mul(trow_sb[:], mrow_sb[:], mrow_sb[:])
            v.tensor_scalar_mul(qrow_sb[:], qrow_sb[:], 1.0 / DM)
            v.tensor_sub(qrow_sb[:], qrow_sb[:], trow_sb[:])
            v.tensor_scalar_add(qrow_sb[:], qrow_sb[:], 1e-5)
            v.reciprocal(trow_sb[:], qrow_sb[:])
            vfence_carry(v, vRow)
            v.wait_ge(aR2.h, aR20 + 1)
            v.tensor_mul(rrow_sb[:], r0row_sb[:], r0row_sb[:])
            v.tensor_mul(rrow_sb[:], rrow_sb[:], qrow_sb[:])
            v.tensor_scalar(rrow_sb[:], rrow_sb[:], -0.5, 1.5, op0=ALU.mult, op1=ALU.add)
            v.tensor_mul(rrow_sb[:], rrow_sb[:], r0row_sb[:])
            vfence_carry(v, vRow)
            for tt in range(NTT):
                v.wait_ge(tBc.h, tBc0 + tt + 1)
                win = slice(tt * 512, (tt + 1) * 512)
                for c in range(NCHUNK):
                    v.tensor_sub(tmpf_sb[:], s_sb[:, c, win], m_ps[:, tt % 2, :])
                    vfence(v)
                    v.tensor_mul(s_sb[:, c, win], tmpf_sb[:], r_ps[:, tt % 2, :])
                    vAp.inc(v)
        blk.vector(_v)

        if not final:
            def _s2(s):
                s = _R(s)
                for tt in range(NTT):
                    for c in range(NCHUNK):
                        i = tt * 8 + c
                        s.wait_ge(vAp.h, vAp0 + i + 1)
                        win = slice(tt * 512, (tt + 1) * 512)
                        s.activation(xb_sb[:, c, win], s_sb[:, c, win], AF.Identity)
                        aCp.inc(s)
            blk.scalar(_s2)


def _emit_router(nc, L, cr):
    (s_sb, rw_sb, ident_sb, combT_sb) = (L["s_sb"], L["rw_sb"], L["ident_sb"], L["combT_sb"])
    tRt, tTr = L["tRt"], L["tTr"]
    vfence = L["vfence"]
    vfence_carry = L["vfence_carry"]
    aEx2 = L["aEx2"]
    vTop1, vTop2, vTop3 = L["vTop1"], L["vTop2"], L["vTop3"]

    ER = cr.enter_context
    pr_sb = ER(nc.sbuf_tensor("rt_pr", [128, NH], F32))
    probs_sb = ER(nc.sbuf_tensor("rt_probs", [128, NH], F32))
    mask_sb = ER(nc.sbuf_tensor("rt_mask", [128, NH], F32))
    pm_sb = ER(nc.sbuf_tensor("rt_pm", [128, NH], F32))
    comb_sb = ER(nc.sbuf_tensor("rt_comb", [128, NH], F32))
    col_sb = ER(nc.sbuf_tensor("rt_cols", [128, 6], F32))
    lg_ps = ER(nc.psum_tensor("rt_lg_ps", [128, 2, 512], F32))
    ct_ps = ER(nc.psum_tensor("rt_ct_ps", [16, 2, 512], F32))

    tRt0, tTr0, aEx20 = tRt.n, tTr.n, aEx2.n
    vTop10, vTop20, vTop30 = vTop1.n, vTop2.n, vTop3.n

    with nc.Block("router") as blk:
        def _t(t):
            t = _R(t)
            for tq in range(NTQ):
                if tq >= 2:
                    t.wait_ge(vTop1.h, vTop10 + tq - 1)
                    t.wait_ge(aEx2.h, aEx20 + tq - 1)
                twin = slice(tq * 128, (tq + 1) * 128)
                for jc in range(8):
                    t.matmul(lg_ps[:, tq % 2, 0:16], s_sb[:, jc, twin], rw_sb[:, jc, :],
                             start=(jc == 0), stop=(jc == 7), skip_group_check=True)
                tRt.inc(t)
                t.wait_ge(vTop2.h, vTop20 + tq + 1)
                if tq >= 2:
                    t.wait_ge(vTop3.h, vTop30 + tq - 1)
                t.transpose(ct_ps[:, tq % 2, 0:128], comb_sb[:], ident_sb[:])
                tTr.inc(t)
        blk.tensor(_t)

        def _s(s):
            s = _R(s)
            for tq in range(NTQ):
                s.wait_ge(vTop1.h, vTop10 + tq + 1)
                s.activation(pr_sb[:], lg_ps[:, tq % 2, 0:16], AF.Exp, bias=col_sb[:, 0:1])
                aEx2.inc(s)
        blk.scalar(_s)

        def _v(v):
            v = _R(v)
            for tq in range(NTQ):
                v.wait_ge(tRt.h, tRt0 + tq + 1)
                v.reduce_max(col_sb[:, 0:1], lg_ps[:, tq % 2, 0:16], axis=AX.X, negate=True)
                vfence_carry(v, vTop1)
                v.wait_ge(aEx2.h, aEx20 + tq + 1)
                v.reduce_sum(col_sb[:, 1:2], pr_sb[:], axis=AX.X)
                vfence(v)
                v.reciprocal(col_sb[:, 1:2], col_sb[:, 1:2])
                vfence(v)
                v.tensor_scalar_mul(probs_sb[:], pr_sb[:], col_sb[:, 1:2])
                vfence(v)
                v.reduce_max(col_sb[:, 2:3], probs_sb[:], axis=AX.X)
                vfence(v)
                v.tensor_single_scalar(mask_sb[:], probs_sb[:], col_sb[:, 2:3], op=ALU.is_ge)
                vfence(v)
                v.scalar_tensor_tensor(pm_sb[:], mask_sb[:], -10.0, probs_sb[:],
                                       op0=ALU.mult, op1=ALU.add)
                vfence(v)
                v.reduce_max(col_sb[:, 3:4], pm_sb[:], axis=AX.X)
                vfence(v)
                v.tensor_add(col_sb[:, 4:5], col_sb[:, 2:3], col_sb[:, 3:4])
                vfence(v)
                v.tensor_scalar_add(col_sb[:, 4:5], col_sb[:, 4:5], 1e-9)
                vfence(v)
                v.reciprocal(col_sb[:, 4:5], col_sb[:, 4:5])
                v.tensor_single_scalar(mask_sb[:], probs_sb[:], col_sb[:, 3:4], op=ALU.is_ge)
                vfence(v)
                v.tensor_mul(comb_sb[:], probs_sb[:], mask_sb[:])
                vfence(v)
                v.tensor_scalar_mul(comb_sb[:], comb_sb[:], col_sb[:, 4:5])
                vfence_carry(v, vTop2)
                v.wait_ge(tTr.h, tTr0 + tq + 1)
                v.tensor_copy(combT_sb[0:16, tq * 128:(tq + 1) * 128], ct_ps[:, tq % 2, 0:128])
                vTop3.inc(v)
        blk.vector(_v)


def _emit_ffn(nc, L, cf):
    (s_sb, xb_sb, ones_sb, combT_sb) = (L["s_sb"], L["xb_sb"], L["ones_sb"], L["combT_sb"])
    (wgr_d, wur_d, wdr_d, wgs_d, wus_d, wds_d) = (L["wgr_d"], L["wur_d"], L["wdr_d"],
                                                  L["wgs_d"], L["wus_d"], L["wds_d"])
    tFg, tCb, tFd = L["tFg"], L["tCb"], L["tFd"]
    gFAp, gFBp = L["gFAp"], L["gFBp"]
    aSl = L["aSl"]
    vCe, vH, vRes2 = L["vCe"], L["vH"], L["vRes2"]

    EF = cf.enter_context
    oh_sb = L["oh_sb"]
    h_sb = EF(nc.sbuf_tensor("ffn_h", [128, NMC, 512], BF16))
    wgu_sb = EF(nc.sbuf_tensor("ffn_wgu", [128, 2, 2, 2, 8, 128], BF16))
    wdo_sb = EF(nc.sbuf_tensor("ffn_wdo", [128, 2, NMC, 128], BF16))
    sg_sb = EF(nc.sbuf_tensor("ffn_sg", [128, 2, 512], BF16))
    u1_sb = EF(nc.sbuf_tensor("ffn_u1", [128, 2, 512], BF16))
    cf_sb = EF(nc.sbuf_tensor("ffn_cf", [128, 2, 512], F32))

    for tt in range(NTT):
        with contextlib.ExitStack() as cpp:
            EP = cpp.enter_context
            g_ps = EP(nc.psum_tensor("ffn_g_ps_%d" % tt, [128, 2, 512], F32))
            u_ps = EP(nc.psum_tensor("ffn_u_ps_%d" % tt, [128, 2, 512], F32))
            c_ps = EP(nc.psum_tensor("ffn_c_ps_%d" % tt, [128, 2, 512], F32))
            d_ps = EP(nc.psum_tensor("ffn_d_ps_%d" % tt, [128, 2, 512], F32))
            win = slice(tt * 512, (tt + 1) * 512)

            gFA0 = [gFAp[0].n, gFAp[1].n]
            gFB0p = [gFBp[0].n, gFBp[1].n]
            tFg0, tCb0, tFd0 = tFg.n, tCb.n, tFd.n
            aSl0, vCe0, vH0, vRes20 = aSl.n, vCe.n, vH.n, vRes2.n

            with nc.Block("ffnA_%d" % tt) as blk:
                def _g(gp):
                    gp = _R(gp)
                    for u in range(NUNIT):
                        if u >= 2:
                            gp.wait_ge(tFg.h, tFg0 + 4 * (u - 1))
                        for gi in range(2):
                            for mcw in range(2):
                                if u < 16:
                                    wd = wgr_d if gi == 0 else wur_d
                                    src = bass.AP(wd, u * DM * 256 + mcw * 128,
                                                  [[256, 128], [128 * 256, 8], [1, 128]])
                                else:
                                    e2, hf = (u - 16) // 2, (u - 16) % 2
                                    wd = wgs_d if gi == 0 else wus_d
                                    src = bass.AP(wd, e2 * DM * 512 + hf * 256 + mcw * 128,
                                                  [[512, 128], [128 * 512, 8], [1, 128]])
                                dst = bass.AP(wgu_sb, (u % 2) * 4096 + gi * 2048 + mcw * 1024,
                                              [[8192, 128], [128, 8], [1, 128]])
                                gp.dma_start(out=dst, in_=src)
                                gFAp[u % 2].inc(gp, 16)
                blk.gpsimd(_g)

                def _t(t):
                    t = _R(t)
                    for u in range(NUNIT):
                        t.wait_ge(gFAp[u % 2].h, gFA0[u % 2] + 64 * (u // 2 + 1))
                        if u < 16:
                            if u >= 2:
                                t.wait_ge(vCe.h, vCe0 + u - 1)
                            t.matmul(c_ps[:, u % 2, :], oh_sb[:, u, :],
                                     combT_sb[0:16, win], start=True, stop=True,
                                     skip_group_check=True)
                            tCb.inc(t)
                        for mcw in range(2):
                            mc = 2 * u + mcw
                            if mc >= 2:
                                t.wait_ge(aSl.h, aSl0 + mc - 1)
                            for jc in range(8):
                                t.matmul(g_ps[:, mc % 2, :], wgu_sb[:, u % 2, 0, mcw, jc, :],
                                         xb_sb[:, jc, win], start=(jc == 0), stop=(jc == 7),
                                         skip_group_check=True)
                            tFg.inc(t)
                            if mc >= 2:
                                t.wait_ge(vH.h, vH0 + mc - 1)
                            for jc in range(8):
                                t.matmul(u_ps[:, mc % 2, :], wgu_sb[:, u % 2, 1, mcw, jc, :],
                                         xb_sb[:, jc, win], start=(jc == 0), stop=(jc == 7),
                                         skip_group_check=True)
                            tFg.inc(t)
                blk.tensor(_t)

                def _s(s):
                    s = _R(s)
                    for mc in range(NMC):
                        s.wait_ge(tFg.h, tFg0 + 2 * mc + 1)
                        if mc >= 2:
                            s.wait_ge(vH.h, vH0 + mc - 1)
                        s.activation(sg_sb[:, mc % 2, :], g_ps[:, mc % 2, :], AF.Silu)
                        aSl.inc(s)
                    L["afence_carry"](s, aSl)
                blk.scalar(_s)

                def _v(v):
                    v = _R(v)
                    for u in range(NUNIT):
                        if u < 16:
                            v.wait_ge(tCb.h, tCb0 + u + 1)
                            v.tensor_copy(cf_sb[:, u % 2, :], c_ps[:, u % 2, :])
                            L["vfence_carry"](v, vCe)
                        for mcw in range(2):
                            mc = 2 * u + mcw
                            v.wait_ge(tFg.h, tFg0 + 2 * mc + 2)
                            v.wait_ge(aSl.h, aSl0 + mc + 2)
                            if u < 16:
                                v.tensor_mul(u1_sb[:, mc % 2, :], u_ps[:, mc % 2, :], cf_sb[:, u % 2, :])
                            else:
                                v.tensor_copy(u1_sb[:, mc % 2, :], u_ps[:, mc % 2, :])
                            L["vfence"](v)
                            v.tensor_mul(h_sb[:, mc, :], sg_sb[:, mc % 2, :], u1_sb[:, mc % 2, :])
                            vH.inc(v)
                blk.vector(_v)

            with nc.Block("ffnB_%d" % tt) as blk:
                def _g(gp):
                    gp = _R(gp)
                    for o in range(8):
                        if o >= 2:
                            gp.wait_ge(tFd.h, tFd0 + o - 1)
                        for e in range(16):
                            src = bass.AP(wdr_d, e * 256 * DM + o * 128,
                                          [[DM, 128], [128 * DM, 2], [1, 128]])
                            gp.dma_start(out=wdo_sb[:, o % 2, 2 * e:2 * e + 2, :], in_=src)
                            gFBp[o % 2].inc(gp, 16)
                        for e2 in range(2):
                            src = bass.AP(wds_d, e2 * 512 * DM + o * 128,
                                          [[DM, 128], [128 * DM, 4], [1, 128]])
                            gp.dma_start(out=wdo_sb[:, o % 2, 32 + 4 * e2:32 + 4 * e2 + 4, :], in_=src)
                            gFBp[o % 2].inc(gp, 16)
                blk.gpsimd(_g)

                def _t(t):
                    t = _R(t)
                    for o in range(8):
                        t.wait_ge(gFBp[o % 2].h, gFB0p[o % 2] + 16 * 18 * (o // 2 + 1))
                        if o >= 2:
                            t.wait_ge(vRes2.h, vRes20 + o - 1)
                        for mc in range(NMC):
                            t.matmul(d_ps[:, o % 2, :], wdo_sb[:, o % 2, mc, :], h_sb[:, mc, :],
                                     start=(mc == 0), stop=(mc == NMC - 1), skip_group_check=True)
                        tFd.inc(t)
                blk.tensor(_t)

                def _v(v):
                    v = _R(v)
                    for o in range(8):
                        v.wait_ge(tFd.h, tFd0 + o + 1)
                        v.tensor_add(s_sb[:, o, win], s_sb[:, o, win], d_ps[:, o % 2, :])
                        vRes2.inc(v)
                blk.vector(_v)



def kernel(**inp):
    x = np.ascontiguousarray(np.asarray(inp["L0_attn_x"], dtype=np.float32))
    w = {k: np.ascontiguousarray(np.asarray(v, dtype=np.float32)) for k, v in inp.items()
         if k != "L0_attn_x"}
    ident = np.eye(128, dtype=np.float32)
    rev = np.eye(128, dtype=np.float32)[::-1].copy()
    J = np.arange(MW)[None, :]
    P = np.arange(128)[:, None]
    idx = np.clip(J - P - 1920, -128, 128) + 128
    bm = np.ascontiguousarray(np.asarray(inp["L0_attn_rel_bias"], dtype=np.float32)[:, idx])
    nc = build_nc()
    common = dict(
        wq=w["L0_attn_Wq"], wk=w["L0_attn_Wk"], wv=w["L0_attn_Wv"], wo=w["L0_attn_Wo"],
        bq=w["L0_attn_bq"], bk=w["L0_attn_bk"], bv=w["L0_attn_bv"], bo=w["L0_attn_bo"],
        relb=w["L0_attn_rel_bias"], rw=w["L0_ffn_router_w"],
        wgr=w["L0_ffn_wg_r"], wur=w["L0_ffn_wu_r"], wdr=w["L0_ffn_wd_r"],
        wgs=w["L0_ffn_wg_s"], wus=w["L0_ffn_wu_s"], wds=w["L0_ffn_wd_s"],
        ident=ident, rev=rev, identb=ident, bm=bm,
        oh=np.kron(np.eye(16, dtype=np.float32), np.ones((1, 128), dtype=np.float32)),
    )
    in_maps = [dict(common, x=np.ascontiguousarray(x[b])) for b in range(8)]
    res = run_bass_kernel_spmd(nc, in_maps, core_ids=list(range(8)))
    out = np.stack([res.results[b]["y"] for b in range(8)], axis=0)
    return out
